# version 21
# baseline (speedup 1.0000x reference)
import numpy as np
import concourse.bass as bass
import concourse.mybir as mybir
from concourse.bass_utils import run_bass_kernel_spmd

F32 = mybir.dt.float32
BF16 = mybir.dt.bfloat16
AF = mybir.ActivationFunctionType
ALU = mybir.AluOpType
AX = mybir.AxisListType.X

NCORES = 8
B = 4
S = 2048
H = 1024
DH = 2048
NE = 8
TT = S // 128
HC = H // 128
HIDC = DH // 128

ENGSEM = {"tensor": "pe", "vector": "dve", "scalar": "act",
          "gpsimd": "gp", "sync": "sp"}


class Emitter:
    def __init__(self):
        self.ops = {e: [] for e in ENGSEM}
        self.counts = {}
        self.last_wait = {}

    def emit(self, engine, fn, waits=(), sem=None, n=1):
        sem = sem or ENGSEM[engine]
        w = []
        for (s, v) in waits:
            if engine == "tensor" and s == "pe":
                continue
            have = self.counts.get(s, 0)
            assert v <= have, f"deadlock: {engine} waits {s}>={v} but built {have}"
            key = (engine, s)
            if self.last_wait.get(key, -1) < v:
                self.last_wait[key] = v
                w.append((s, v))
        self.counts[sem] = self.counts.get(sem, 0) + n
        tok = (sem, self.counts[sem])
        self.ops[engine].append((w, fn, (sem, n)))
        return tok


class Buf:

    def __init__(self, name):
        self.name = name
        self.wtok = None
        self.rtoks = []
        self.aliases = []

    def r_waits(self):
        return [self.wtok] if self.wtok else []

    def w_waits(self):
        ws = []
        for t in [self] + self.aliases:
            if t.wtok:
                ws.append(t.wtok)
            ws += t.rtoks
        return ws

    def did_read(self, tok):
        self.rtoks.append(tok)

    def did_write(self, tok):
        self.wtok = tok
        self.rtoks = []
        for t in self.aliases:
            t.wtok = None
            t.rtoks = []


def link_alias(a, b):
    if b not in a.aliases:
        a.aliases.append(b)
    if a not in b.aliases:
        b.aliases.append(a)


def build_program():
    nc = bass.Bass("TRN2", target_bir_lowering=False)
    em = Emitter()

    x_d = nc.dram_tensor("x", [B, S, H], F32, kind="ExternalInput").ap()
    wq_d = nc.dram_tensor("wqT", [H, H], F32, kind="ExternalInput").ap()
    wk_d = nc.dram_tensor("wkT", [H, H], F32, kind="ExternalInput").ap()
    wv_d = nc.dram_tensor("wvT", [H, H], F32, kind="ExternalInput").ap()
    wo_d = nc.dram_tensor("woT", [H, H], F32, kind="ExternalInput").ap()
    gw_d = nc.dram_tensor("gw", [H, NE], F32, kind="ExternalInput").ap()
    w1_d = nc.dram_tensor("w1", [NE, H, DH], F32, kind="ExternalInput").ap()
    b1_d = nc.dram_tensor("b1", [NE, DH], F32, kind="ExternalInput").ap()
    w2_d = nc.dram_tensor("w2", [NE, DH, H], F32, kind="ExternalInput").ap()
    b2_d = nc.dram_tensor("b2", [NE, H], F32, kind="ExternalInput").ap()
    out_d = nc.dram_tensor("out", [B, S, H], F32, kind="ExternalOutput").ap()
    sscr_d = nc.dram_tensor("s_scr", [S, H], BF16, kind="Internal").ap()
    w1s_d = nc.dram_tensor("w1s", [NE, 128, 16384], BF16, kind="Internal").ap()
    w2s_d = nc.dram_tensor("w2s", [NE, 128, 16384], BF16, kind="Internal").ap()

    from contextlib import ExitStack
    with ExitStack() as ctx:
        sb = lambda name, shape, dt: ctx.enter_context(nc.sbuf_tensor(name, shape, dt))
        au0 = sb("au0", [128, 16384], BF16)
        au1 = sb("au1", [128, 16384], BF16)
        xn_fm = sb("xn_fm", [128, HC, S], BF16)
        reg = sb("reg", [128, 40960], BF16)
        attn_sb = sb("attn_sb", [128, 2048], BF16)
        attnT_sb = sb("attnT_sb", [128, 2048], BF16)
        q_sb = sb("q_sb", [128, 1024], BF16)
        ctxT_own = sb("ctxT_own", [128, 1024], BF16)
        s_sb = sb("s_sb", [128, 2, 1024], BF16)
        xa = sb("xa", [128, 3, 1024], F32)
        cw_sb = sb("cw_sb", [128, TT * NE], F32)
        b2_sb = sb("b2_sb", [8, H], BF16)
        gw_sb = sb("gw_sb", [128, HC, NE], BF16)
        b1c_sb = sb("b1c_sb", [128, NE * HIDC], F32)
        id_bf = sb("id_bf", [128, 128], BF16)
        id_f32 = sb("id_f32", [128, 8], F32)
        msum = sb("msum", [128, TT], F32)
        sqs = sb("sqs", [128, TT], F32)
        nmean = msum
        rstd = sb("rstd", [128, TT], F32)
        nmr = sb("nmr", [128, TT], F32)
        gsc = sb("gsc", [128, 8], F32)
        e_sb = sb("e_sb", [128, NE], F32)
        eq_sb = sb("eq_sb", [128, NE], F32)
        mk_sb = eq_sb
        cwbf_sb = sb("cwbf_sb", [128, 2, NE], BF16)
        asc = sb("asc", [128, 12], F32)
        c32_sb = gsc[:, 7:8]

        rem = nc.sbuf_bytes_remaining
        assert rem >= 0, f"SBUF over budget by {-rem} bytes"

        pb = [ctx.enter_context(nc.psum_tensor(f"pb{i}", [128, 512], F32))
              for i in range(8)]
        pbf = [p.ap().bitcast(BF16) for p in pb]

        sems = {name: ctx.enter_context(nc.semaphore(name))
                for name in ["pe", "dve", "act", "gp", "sp",
                             "wa0", "wa1", "wb0", "wb1",
                             "lqa0", "lqa1", "lqa2",
                             "lqs0", "lqs1",
                             "sta0", "sta1", "sta2", "sts0", "sts1",
                             "w_b2", "w_gw", "w_b1", "wc"]}

        k_fm = reg[:, 0:16384].rearrange("p (tq hc t) -> p tq hc t", tq=4, hc=HC)
        v_tm = reg[:, 16384:32768].rearrange("p (tc h) -> p tc h", tc=TT)
        acc_f = reg.ap().bitcast(F32)
        acc = acc_f[:, 0:16384].rearrange("p (tt o) -> p tt o", tt=TT)
        h_fm = reg[:, 32768:40960].rearrange("p (hd t) -> p hd t", hd=HIDC)
        sq_f = attn_sb.ap().bitcast(F32)
        g_f = attnT_sb.ap().bitcast(F32)
        ctx_sb = attn_sb[:, 0:1024]
        ctxT_sb = ctxT_own.ap()
        cwtj_sb = ctxT_own[0:8, 0:128]

        bQ = Buf("q")
        bHa, bHb = Buf("ha"), Buf("hb")
        bKq = [Buf(f"kq{i}") for i in range(4)]
        bVp = [Buf(f"vp{i}") for i in range(8)]
        bAcc = [Buf(f"acc{i}") for i in range(TT)]
        for i in range(8):
            link_alias(bAcc[i], bKq[i // 2])
        for i in range(8, TT):
            link_alias(bAcc[i], bVp[i - 8])
        bXnQ = [Buf(f"xn{i}") for i in range(4)]
        bAtL, bAtR = Buf("attnL"), Buf("attnR")
        bAttnT = Buf("attnT")
        bAtTA, bAtTB = Buf("attnTA"), Buf("attnTB")
        link_alias(bAtTA, bAttnT)
        link_alias(bAtTB, bAttnT)
        bCtxTo = Buf("ctxTo")
        bS = [Buf("s0"), Buf("s1")]
        bXa = [Buf(f"xa{i}") for i in range(3)]
        bCw, bCwTj = Buf("cw"), Buf("cwTj")
        link_alias(bCwTj, bCtxTo)
        bB2, bGw, bB1c = Buf("b2"), Buf("gw"), Buf("b1c")
        bIdb, bIdf = Buf("idb"), Buf("idf")
        bAuA, bAuB = Buf("au0a"), Buf("au0b")
        bAu1a, bAu1b = Buf("au1a"), Buf("au1b")
        bStat = Buf("stat")
        bGsc = Buf("gsc")
        bCwb = [Buf("cwb0"), Buf("cwb1")]
        bAsc = Buf("asc")
        bPB = [Buf(f"pb{i}") for i in range(8)]
        bSscr = [Buf(f"s_scr{i}") for i in range(TT)]
        bW1s = [Buf(f"w1s{i}") for i in range(NE)]
        bW2s = [Buf(f"w2s{i}") for i in range(NE)]

        def step(engine, fn, reads=(), writes=(), extra=(), sem=None, n=1):
            waits = list(extra)
            for bf in reads:
                waits += bf.r_waits()
            for bf in writes:
                waits += bf.w_waits()
            tok = em.emit(engine, fn, waits, sem=sem, n=n)
            for bf in reads:
                bf.did_read(tok)
            for bf in writes:
                bf.did_write(tok)
            return tok

        nc.const_aps.register(F32, 1.0 / 32.0, c32_sb)
        step("gpsimd", lambda g: g.memset(c32_sb, 1.0 / 32.0),
             writes=[bIdf])
        step("gpsimd", lambda g: g.memset(id_bf[:, :], 0.0), writes=[bIdb])
        step("gpsimd", lambda g: g.affine_select(
            out=id_bf[:, :], in_=id_bf[:, :], compare_op=ALU.not_equal,
            fill=1.0, base=0, pattern=[[-1, 128]], channel_multiplier=1),
            reads=[bIdb], writes=[bIdb])
        step("gpsimd", lambda g: g.memset(id_f32[:, :], 0.0), writes=[bIdf])
        step("gpsimd", lambda g: g.affine_select(
            out=id_f32[:, :], in_=id_f32[:, :], compare_op=ALU.not_equal,
            fill=1.0, base=0, pattern=[[-1, 8]], channel_multiplier=1),
            reads=[bIdf], writes=[bIdf])

        step("gpsimd", lambda g: g.dma_start(out=b2_sb[:, :], in_=b2_d),
             writes=[bB2], sem="w_b2", n=16)
        step("gpsimd", lambda g: g.dma_start(
            out=gw_sb[:, :, :],
            in_=gw_d.rearrange("(hc p) e -> p hc e", p=128)),
            writes=[bGw], sem="w_gw", n=16)
        b1nat = xa.ap().rearrange("p three h -> p (three h)")[0:8, 0:2048]
        step("gpsimd", lambda g: g.dma_start(out=b1nat, in_=b1_d),
             writes=[bXa[0], bXa[1]], sem="w_b1", n=16)

        def f_b1t(t):
            last = None
            for hd in range(HIDC):
                last = t.transpose(pb[6][:, hd * 8:(hd + 1) * 8],
                                   b1nat[0:8, hd * 128:(hd + 1) * 128],
                                   id_f32[0:8, 0:8])
            return last
        step("tensor", f_b1t, reads=[bXa[0], bXa[1], bIdf], writes=[bPB[6]])
        step("vector", lambda v: v.tensor_copy(
            b1c_sb[:, :].rearrange("p (e hd) -> p e hd", e=NE),
            pb[6][:, 0:128].rearrange("p (hd e) -> p e hd", hd=HIDC)),
            reads=[bPB[6]], writes=[bB1c])

        def warena(dst, src):
            return lambda g: g.dma_start(
                out=dst.rearrange("p (hc o) -> p hc o", hc=HC),
                in_=src.rearrange("(hc p) o -> p hc o", p=128))
        step("gpsimd", warena(au0[:, 0:8192], wq_d), writes=[bAuA], sem="wa0", n=16)
        step("gpsimd", warena(au0[:, 8192:16384], wk_d), writes=[bAuB], sem="wa1", n=16)
        step("gpsimd", warena(au1[:, 0:8192], wv_d), writes=[bAu1a], sem="wb0", n=16)
        step("gpsimd", warena(au1[:, 8192:16384], wo_d), writes=[bAu1b], sem="wb1", n=16)
        for ci in range(2 * NE):
            e = ci % NE
            if ci < NE:
                fn = (lambda e=e: lambda g: g.dma_start(
                    out=w1s_d[e].rearrange("p (hc d) -> p hc d", hc=HC),
                    in_=w1_d[e].rearrange("(hc p) d -> p hc d", p=128)))()
                trk = bW1s[e]
            else:
                fn = (lambda e=e: lambda g: g.dma_start(
                    out=w2s_d[e].rearrange("p (hd o) -> p hd o", hd=HIDC),
                    in_=w2_d[e].rearrange("(hd p) o -> p hd o", p=128)))()
                trk = bW2s[e]
            step("gpsimd", fn, writes=[trk], extra=[("wc", 16 * ci)],
                 sem="wc", n=16)

        def ln_stats_tile(b, i, sl):
            step("sync", (lambda: lambda sp: sp.dma_start(
                out=xa[:, sl, :], in_=x_d[b, i * 128:(i + 1) * 128, :]))(),
                writes=[bXa[sl]], sem="lqa%d" % sl, n=16)
            step("vector", (lambda: lambda v: v.reduce_sum(
                msum[:, i:i + 1], xa[:, sl, :], AX))(),
                reads=[bXa[sl]], writes=[bStat])
            step("scalar", (lambda: lambda s: s.activation(
                sq_f[:, :], xa[:, sl, :], AF.Square,
                accum_out=sqs[:, i:i + 1]))(),
                reads=[bXa[sl]], writes=[bAtL, bAtR, bStat])

        def stats_math():
            step("vector", lambda v: v.tensor_scalar(
                nmean[:, :], msum[:, :], -1.0 / H, None, ALU.mult),
                reads=[bStat], writes=[bStat])
            step("vector", lambda v: v.tensor_mul(
                rstd[:, :], nmean[:, :], nmean[:, :]),
                reads=[bStat], writes=[bStat])
            step("vector", lambda v: v.scalar_tensor_tensor(
                rstd[:, :], sqs[:, :], 1.0 / H, rstd[:, :],
                ALU.mult, ALU.subtract),
                reads=[bStat], writes=[bStat])
            step("vector", lambda v: v.tensor_scalar_add(
                rstd[:, :], rstd[:, :], 1e-5),
                reads=[bStat], writes=[bStat])
            step("scalar", lambda s: s.activation(
                sqs[:, :], rstd[:, :], AF.Sqrt),
                reads=[bStat], writes=[bStat])
            step("vector", lambda v: v.reciprocal(rstd[:, :], sqs[:, :]),
                 reads=[bStat], writes=[bStat])
            step("vector", lambda v: v.tensor_mul(
                nmr[:, :], nmean[:, :], rstd[:, :]),
                reads=[bStat], writes=[bStat])

        def xn_pre(b, i, sl):
            half = i % 2
            xntm = attn_sb[:, half * 1024:(half + 1) * 1024]
            bHalf = bAtL if half == 0 else bAtR
            step("sync", (lambda: lambda sp: sp.dma_start(
                out=xa[:, sl, :], in_=x_d[b, i * 128:(i + 1) * 128, :]))(),
                writes=[bXa[sl]], sem="lqa%d" % sl, n=16)
            step("vector", (lambda: lambda v: v.tensor_scalar(
                xntm, xa[:, sl, :], rstd[:, i:i + 1], nmr[:, i:i + 1],
                ALU.mult, ALU.add))(),
                reads=[bXa[sl], bStat], writes=[bHalf])

        def xn_post(b, i):
            half = i % 2
            xntm = attn_sb[:, half * 1024:(half + 1) * 1024]
            bHalf = bAtL if half == 0 else bAtR
            bank = 6 + half

            def f_xt(t):
                last = None
                for hc in range(HC):
                    last = t.transpose(pbf[bank][:, hc * 128:(hc + 1) * 128],
                                       xntm[:, hc * 128:(hc + 1) * 128],
                                       id_bf[:, :])
                return last
            step("tensor", f_xt, reads=[bHalf, bIdb], writes=[bPB[bank]])
            step("vector", (lambda: lambda v: v.tensor_copy(
                xn_fm[:, :, i * 128:(i + 1) * 128], pbf[bank][:, 0:1024]
                .rearrange("p (hc t) -> p hc t", hc=HC)))(),
                reads=[bPB[bank]], writes=[bXnQ[i // 4]])

        def xn_tile(b, i, sl):
            xn_pre(b, i, sl)
            xn_post(b, i)

        def k_quarter(tq):
            for oc in range(HC):
                bank = oc % 2

                def f_mm(t, oc=oc, bank=bank):
                    last = None
                    for hc in range(HC):
                        last = t.matmul(
                            pb[bank][:, :],
                            au0[:, 8192 + hc * 1024 + oc * 128:
                                8192 + hc * 1024 + oc * 128 + 128],
                            xn_fm[:, hc, tq * 512:(tq + 1) * 512],
                            start=(hc == 0), stop=(hc == HC - 1))
                    return last
                step("tensor", f_mm, reads=[bAuB, bXnQ[tq]], writes=[bPB[bank]])
                step("scalar", (lambda oc=oc, bank=bank: lambda s: s.copy(
                    k_fm[:, tq, oc, :], pb[bank][:, :]))(),
                    reads=[bPB[bank]], writes=[bKq[tq]])

        def v_tile(tt):
            for half in range(2):
                bank = 2 + half

                def f_mm(t, half=half, bank=bank):
                    last = None
                    for hc in range(HC):
                        last = t.matmul(
                            pb[bank][:, :],
                            xn_fm[:, hc, tt * 128:(tt + 1) * 128],
                            au1[:, hc * 1024 + half * 512:
                                hc * 1024 + half * 512 + 512],
                            start=(hc == 0), stop=(hc == HC - 1))
                    return last
                step("tensor", f_mm, reads=[bAu1a, bXnQ[tt // 4]],
                     writes=[bPB[bank]])
                step("scalar", (lambda half=half, bank=bank: lambda s: s.copy(
                    v_tm[:, tt, half * 512:(half + 1) * 512], pb[bank][:, :]))(),
                    reads=[bPB[bank]], writes=[bVp[tt // 2]])

        def gate_tile(tt):
            gb = 6 + tt % 2

            def f_gmm(t):
                last = None
                for hc in range(HC):
                    last = t.matmul(
                        pb[gb][:, 0:NE],
                        xn_fm[:, hc, tt * 128:(tt + 1) * 128],
                        gw_sb[:, hc, :],
                        start=(hc == 0), stop=(hc == HC - 1))
                return last
            step("tensor", f_gmm, reads=[bGw, bXnQ[tt // 4]], writes=[bPB[gb]])
            step("vector", (lambda gb=gb: lambda v: v.reduce_max(
                gsc[:, 0:1], pb[gb][:, 0:NE], AX, negate=True))(),
                reads=[bPB[gb]], writes=[bGsc])
            step("scalar", (lambda gb=gb: lambda s: s.activation(
                e_sb[:, :], pb[gb][:, 0:NE], AF.Exp,
                bias=gsc[:, 0:1], accum_out=gsc[:, 1:2]))(),
                reads=[bPB[gb], bGsc], writes=[bGsc])
            step("vector", lambda v: v.reduce_max(gsc[:, 2:3], e_sb[:, :], AX),
                 reads=[bGsc], writes=[bGsc])
            step("vector", lambda v: v.tensor_scalar(
                eq_sb[:, :], e_sb[:, :], gsc[:, 2:3], None, ALU.is_ge),
                reads=[bGsc], writes=[bGsc])
            step("vector", lambda v: v.scalar_tensor_tensor(
                mk_sb[:, :], eq_sb[:, :], -1e30, e_sb[:, :], ALU.mult, ALU.add),
                reads=[bGsc], writes=[bGsc])
            step("vector", lambda v: v.reduce_max(gsc[:, 3:4], mk_sb[:, :], AX),
                 reads=[bGsc], writes=[bGsc])
            step("vector", lambda v: v.tensor_scalar(
                eq_sb[:, :], e_sb[:, :], gsc[:, 3:4], None, ALU.is_ge),
                reads=[bGsc], writes=[bGsc])
            step("vector", lambda v: v.scalar_tensor_tensor(
                gsc[:, 4:5], gsc[:, 1:2], 1e-8, gsc[:, 2:3], ALU.mult, ALU.add),
                reads=[bGsc], writes=[bGsc])
            step("vector", lambda v: v.tensor_add(
                gsc[:, 5:6], gsc[:, 4:5], gsc[:, 3:4]),
                reads=[bGsc], writes=[bGsc])
            step("vector", lambda v: v.reciprocal(gsc[:, 6:7], gsc[:, 5:6]),
                 reads=[bGsc], writes=[bGsc])
            step("vector", (lambda: lambda v: v.tensor_scalar(
                cw_sb[:, tt * NE:(tt + 1) * NE], e_sb[:, :], gsc[:, 6:7],
                None, ALU.mult))(),
                reads=[bGsc], writes=[bCw])
            step("vector", (lambda: lambda v: v.tensor_mul(
                cw_sb[:, tt * NE:(tt + 1) * NE],
                cw_sb[:, tt * NE:(tt + 1) * NE], eq_sb[:, :]))(),
                reads=[bGsc, bCw], writes=[bCw])

        def attn_qjit(qt):
            def f_q(t):
                last = None
                for oc in range(HC):
                    bank = 4 + oc // 4
                    for hc in range(HC):
                        last = t.matmul(
                            pb[bank][:, (oc % 4) * 128:(oc % 4) * 128 + 128],
                            au0[:, hc * 1024 + oc * 128:
                                hc * 1024 + oc * 128 + 128],
                            xn_fm[:, hc, qt * 128:(qt + 1) * 128],
                            start=(hc == 0), stop=(hc == HC - 1))
                return last
            step("tensor", f_q, reads=[bAuA, bXnQ[qt // 4]],
                 writes=[bPB[4], bPB[5]])
            for bk in (4, 5):
                step("scalar", (lambda bk=bk: lambda s: s.activation(
                    q_sb[:, (bk - 4) * 512:(bk - 4) * 512 + 512],
                    pb[bk][:, :], AF.Copy, scale=1.0 / 32.0))(),
                    reads=[bPB[bk], bIdf], writes=[bQ])

        def attn_scores(qt):
            def f_sc(lo):
                def f(t):
                    last = None
                    for kq in (lo, lo + 1):
                        for oc in range(HC):
                            last = t.matmul(
                                pb[kq][:, :],
                                q_sb[:, oc * 128:(oc + 1) * 128],
                                k_fm[:, kq, oc, :],
                                start=(oc == 0), stop=(oc == HC - 1))
                    return last
                return f
            step("tensor", f_sc(0), reads=[bQ, bKq[0], bKq[1]],
                 writes=[bPB[0], bPB[1]])
            for kq in (0, 1):
                step("vector", (lambda kq=kq: lambda v: v.reduce_max(
                    asc[:, kq:kq + 1], pb[kq][:, :], AX, negate=True))(),
                    reads=[bPB[kq]], writes=[bAsc])
            step("vector", lambda v: v.tensor_reduce(
                asc[:, 4:5], asc[:, 0:2], AX, ALU.min),
                reads=[bAsc], writes=[bAsc])
            for kq in (0, 1):
                step("scalar", (lambda kq=kq: lambda s: s.activation(
                    attn_sb[:, kq * 512:(kq + 1) * 512], pb[kq][:, :],
                    AF.Exp, bias=asc[:, 4:5],
                    accum_out=asc[:, 5 + kq:6 + kq]))(),
                    reads=[bPB[kq], bAsc], writes=[bAtL, bAsc])
            step("tensor", f_sc(2), reads=[bQ, bKq[2], bKq[3]],
                 writes=[bPB[2], bPB[3]])
            for kq in (2, 3):
                step("scalar", (lambda kq=kq: lambda s: s.activation(
                    attn_sb[:, kq * 512:(kq + 1) * 512], pb[kq][:, :],
                    AF.Exp, bias=asc[:, 4:5],
                    accum_out=asc[:, 5 + kq:6 + kq]))(),
                    reads=[bPB[kq], bAsc], writes=[bAtR, bAsc])

        def attn_rest(qt):
            step("vector", lambda v: v.tensor_reduce(
                asc[:, 9:10], asc[:, 5:9], AX, ALU.add),
                reads=[bAsc], writes=[bAsc])
            step("vector", lambda v: v.reciprocal(asc[:, 10:11], asc[:, 9:10]),
                 reads=[bAsc], writes=[bAsc])
            for bk, lo in ((4, 0), (5, 8)):
                bHalf = bAtL if lo == 0 else bAtR

                def f_tr(t, bk=bk, lo=lo):
                    last = None
                    for c in range(8):
                        ktc = lo + c
                        last = t.transpose(
                            pbf[bk][:, c * 128:(c + 1) * 128],
                            attn_sb[:, ktc * 128:(ktc + 1) * 128],
                            id_bf[:, :])
                    return last
                step("tensor", f_tr, reads=[bHalf, bIdb], writes=[bPB[bk]])
                step("scalar", (lambda bk=bk, lo=lo: lambda s: s.copy(
                    attnT_sb[:, lo * 128:(lo + 8) * 128], pbf[bk][:, 0:1024]))(),
                    reads=[bPB[bk]], writes=[bAtTA if lo == 0 else bAtTB])

            def f_ctx(lo):
                def f(t):
                    last = None
                    for half in range(2):
                        for ktc in range(lo, lo + 8):
                            last = t.matmul(
                                pb[6 + half][:, :],
                                attnT_sb[:, ktc * 128:(ktc + 1) * 128],
                                v_tm[:, ktc, half * 512:(half + 1) * 512],
                                start=(ktc == 0), stop=(ktc == TT - 1))
                    return last
                return f
            step("tensor", f_ctx(0), reads=[bAtTA] + bVp, writes=[bPB[6], bPB[7]])
            step("tensor", f_ctx(8), reads=[bAtTB] + bVp, writes=[bPB[6], bPB[7]])
            for half in range(2):
                step("vector", (lambda half=half: lambda v: v.tensor_scalar(
                    ctx_sb[:, half * 512:(half + 1) * 512],
                    pb[6 + half][:, :], asc[:, 10:11], None, ALU.mult))(),
                    reads=[bPB[6 + half], bAsc], writes=[bAtL])

            if qt < TT - 1:
                attn_qjit(qt + 1)

            def f_ctr(t):
                last = None
                for hc in range(HC):
                    last = t.transpose(
                        pbf[6][:, hc * 128:(hc + 1) * 128],
                        ctx_sb[:, hc * 128:(hc + 1) * 128], id_bf[:, :])
                return last
            step("tensor", f_ctr, reads=[bAtL, bIdb], writes=[bPB[6]])
            step("scalar", lambda s: s.copy(
                ctxT_sb[:, :], pbf[6][:, 0:1024]),
                reads=[bPB[6]], writes=[bCtxTo])

            if qt < TT - 1:
                attn_scores(qt + 1)

            def f_s(t):
                last = None
                for half in range(2):
                    bank = (7, 4)[half]
                    for hc in range(HC):
                        last = t.matmul(
                            pb[bank][:, :],
                            ctxT_sb[:, hc * 128:(hc + 1) * 128],
                            au1[:, 8192 + hc * 1024 + half * 512:
                                8192 + hc * 1024 + half * 512 + 512],
                            start=(hc == 0), stop=(hc == HC - 1))
                return last
            step("tensor", f_s, reads=[bCtxTo, bAu1b], writes=[bPB[7], bPB[4]])
            sl = qt % 2
            for half in range(2):
                bank = (7, 4)[half]
                step("vector", (lambda half=half, bank=bank: lambda v: v.tensor_copy(
                    s_sb[:, sl, half * 512:(half + 1) * 512], pb[bank][:, :]))(),
                    reads=[bPB[bank]], writes=[bS[sl]])
            step("sync", (lambda: lambda sp: sp.dma_start(
                out=sscr_d[qt * 128:(qt + 1) * 128, :], in_=s_sb[:, sl, :]))(),
                reads=[bS[sl]], writes=[bSscr[qt]], sem="sts%d" % sl, n=16)

        def cw_copy(tt):
            step("vector", (lambda: lambda v: v.tensor_copy(
                cwbf_sb[:, tt % 2, :], cw_sb[:, tt * NE:(tt + 1) * NE]))(),
                reads=[bCw], writes=[bCwb[tt % 2]])

        def accinit_tile(b, tt):
            sl = tt % 3
            step("tensor", (lambda: lambda t: t.transpose(
                pbf[6][:, 0:128][0:NE, :], cwbf_sb[:, tt % 2, :],
                id_bf[:, :]))(),
                reads=[bCwb[tt % 2], bIdb], writes=[bPB[6]])
            step("vector", lambda v: v.tensor_copy(
                cwtj_sb[:, :], pbf[6][0:NE, 0:128]),
                reads=[bPB[6]], writes=[bCwTj])

            def f_b2m(t):
                t.matmul(pb[6][:, :], cwtj_sb[0:NE, :],
                         b2_sb[0:NE, 0:512], start=True, stop=True)
                return t.matmul(pb[7][:, :], cwtj_sb[0:NE, :],
                                b2_sb[0:NE, 512:1024], start=True, stop=True)
            step("tensor", f_b2m, reads=[bCwTj, bB2], writes=[bPB[6], bPB[7]])
            step("sync", (lambda: lambda sp: sp.dma_start(
                out=xa[:, sl, :], in_=x_d[b, tt * 128:(tt + 1) * 128, :]))(),
                writes=[bXa[sl]], sem="lqa%d" % sl, n=16)
            step("vector", (lambda: lambda v: v.tensor_scalar(
                acc[:, tt, :], xa[:, sl, :], rstd[:, tt:tt + 1],
                nmr[:, tt:tt + 1], ALU.mult, ALU.add))(),
                reads=[bXa[sl], bStat], writes=[bAcc[tt]])
            for half in range(2):
                step("vector", (lambda half=half: lambda v: v.tensor_add(
                    acc[:, tt, half * 512:(half + 1) * 512],
                    acc[:, tt, half * 512:(half + 1) * 512],
                    pb[6 + half][:, :]))(),
                    reads=[bPB[6 + half], bAcc[tt]], writes=[bAcc[tt]])
            if tt < TT - 1:
                cw_copy(tt + 1)

        def m1_group(e, slab, hd):
            bank = (slab * HIDC + hd) % 2

            def f_m1(t):
                last = None
                for hc in range(HC):
                    last = t.matmul(
                        pb[bank][:, :],
                        au0[:, hc * 2048 + hd * 128:hc * 2048 + hd * 128 + 128],
                        xn_fm[:, hc, slab * 512:(slab + 1) * 512],
                        start=(hc == 0), stop=(hc == HC - 1))
                return last
            step("tensor", f_m1, reads=[bAuA, bAuB, bXnQ[slab]],
                 writes=[bPB[bank]])
            trk = bHa if hd < 8 else bHb
            if hd == HIDC - 1:
                step("vector", (lambda: lambda v: v.tensor_scalar(
                    h_fm[:, hd, :], pb[bank][:, :],
                    b1c_sb[:, e * HIDC + hd:e * HIDC + hd + 1], 0.0,
                    ALU.add, ALU.max))(),
                    reads=[bPB[bank], bB1c], writes=[trk])
            else:
                step("scalar", (lambda: lambda s: s.activation(
                    h_fm[:, hd, :], pb[bank][:, :], AF.Relu,
                    bias=b1c_sb[:, e * HIDC + hd:e * HIDC + hd + 1]))(),
                    reads=[bPB[bank], bB1c], writes=[trk])

        def m2_tile(e, tt):
            if 1 <= e <= 6:
                pr = (2, 4, 6)[tt % 3]
            else:
                pr = 2 + 2 * (tt % 2)

            def f_m2(lo):
                def f(t):
                    last = None
                    for half in range(2):
                        for hd in range(lo, lo + 8):
                            last = t.matmul(
                                pb[pr + half][:, :],
                                h_fm[:, hd, (tt % 4) * 128:(tt % 4) * 128 + 128],
                                au1[:, hd * 1024 + half * 512:
                                    hd * 1024 + half * 512 + 512],
                                start=(hd == 0), stop=(hd == HIDC - 1))
                    return last
                return f
            step("tensor", f_m2(0), reads=[bHa, bAu1a, bAu1b],
                 writes=[bPB[pr], bPB[pr + 1]])
            step("tensor", f_m2(8), reads=[bHb, bAu1a, bAu1b],
                 writes=[bPB[pr], bPB[pr + 1]])
            for half in range(2):
                step("vector", (lambda half=half: lambda v: v.scalar_tensor_tensor(
                    acc[:, tt, half * 512:(half + 1) * 512],
                    pb[pr + half][:, :],
                    cw_sb[:, tt * NE + e:tt * NE + e + 1],
                    acc[:, tt, half * 512:(half + 1) * 512],
                    ALU.mult, ALU.add))(),
                    reads=[bPB[pr + half], bCw, bAcc[tt]],
                    writes=[bAcc[tt]])

        def combine_tile(b, tt, sl):
            ss = tt % 2
            gbuf = g_f if tt % 2 == 0 else sq_f
            gtrk = [bAttnT] if tt % 2 == 0 else [bAtL, bAtR]
            step("sync", (lambda: lambda sp: sp.dma_start(
                out=xa[:, sl, :], in_=x_d[b, tt * 128:(tt + 1) * 128, :]))(),
                writes=[bXa[sl]], sem="lqa%d" % sl, n=16)
            step("sync", (lambda: lambda sp: sp.dma_start(
                out=s_sb[:, ss, :], in_=sscr_d[tt * 128:(tt + 1) * 128, :]))(),
                reads=[bSscr[tt]], writes=[bS[ss]], sem="lqs%d" % ss, n=16)
            step("scalar", (lambda: lambda s: s.activation(
                gbuf[:, :], acc[:, tt, :], AF.Silu))(),
                reads=[bAcc[tt]], writes=gtrk)
            step("vector", (lambda: lambda v: v.tensor_mul(
                gbuf[:, :], gbuf[:, :], s_sb[:, ss, :]))(),
                reads=[bS[ss]] + gtrk, writes=gtrk)
            step("vector", (lambda: lambda v: v.tensor_add(
                xa[:, sl, :], xa[:, sl, :], gbuf[:, :]))(),
                extra=[t for bf in gtrk for t in bf.r_waits()],
                reads=gtrk + [bXa[sl]], writes=[bXa[sl]])
            step("sync", (lambda: lambda sp: sp.dma_start(
                out=out_d[b, tt * 128:(tt + 1) * 128, :], in_=xa[:, sl, :]))(),
                reads=[bXa[sl]], sem="sta%d" % sl, n=16)

        def load_w1_half(b, e, lohi):
            half = (au0[:, 0:8192], au0[:, 8192:16384])[lohi]
            trk = (bAuA, bAuB)[lohi]
            cols = (slice(0, 8192), slice(8192, 16384))[lohi]
            step("gpsimd", (lambda: lambda g: g.dma_start(
                out=half, in_=w1s_d[e][:, cols]))(),
                reads=[bW1s[e]], writes=[trk], sem="wa%d" % lohi, n=16)

        def load_w1_full(b, e):
            step("gpsimd", (lambda: lambda g: g.dma_start(
                out=au0[:, :], in_=w1s_d[e]))(),
                reads=[bW1s[e]], writes=[bAuA, bAuB], sem="wa0", n=16)

        def load_w2(b, e):
            step("gpsimd", (lambda: lambda g: g.dma_start(
                out=au1[:, :], in_=w2s_d[e]))(),
                reads=[bW2s[e]], writes=[bAu1a, bAu1b], sem="wb0", n=16)

        for i in range(TT):
            step("sync", (lambda i=i: lambda sp: sp.dma_start(
                out=acc[:, i, :], in_=x_d[0, i * 128:(i + 1) * 128, :]))(),
                writes=[bAcc[i]],
                extra=[("lqa%d" % (i % 3), 16 * (i // 3))],
                sem="lqa%d" % (i % 3), n=16)
            step("vector", (lambda i=i: lambda v: v.reduce_sum(
                msum[:, i:i + 1], acc[:, i, :], AX))(),
                reads=[bAcc[i]], writes=[bStat])
            step("scalar", (lambda i=i: lambda s: s.activation(
                sq_f[:, :], acc[:, i, :], AF.Square,
                accum_out=sqs[:, i:i + 1]))(),
                reads=[bAcc[i]], writes=[bAtL, bAtR, bStat])
        stats_math()
        for i in range(TT):
            half = i % 2
            xntm0 = attn_sb[:, half * 1024:(half + 1) * 1024]
            step("vector", (lambda i=i, xntm0=xntm0: lambda v: v.tensor_scalar(
                xntm0, acc[:, i, :], rstd[:, i:i + 1], nmr[:, i:i + 1],
                ALU.mult, ALU.add))(),
                reads=[bAcc[i], bStat], writes=[bAtL if half == 0 else bAtR])
            xn_post(0, i)
        for tq in range(4):
            k_quarter(tq)
        for tt in range(TT):
            v_tile(tt)
            gate_tile(tt)

        for b in range(B):
            load_w1_half(b, 0, 1)
            if b == 0:
                attn_qjit(0)
                attn_scores(0)
            for qt in range(TT):
                attn_rest(qt)
            load_w1_half(b, 0, 0)
            load_w2(b, 0)
            cw_copy(0)

            for e in range(NE):
                for slab in range(4):
                    for hd in range(HIDC):
                        m1_group(e, slab, hd)
                        if e == 0 and hd % 4 == 3:
                            accinit_tile(b, slab * 4 + hd // 4)
                    if slab == 3 and e < NE - 1:
                        load_w1_full(b, e + 1)
                    if slab == 3 and e == NE - 1 and b < B - 1:
                        step("gpsimd", warena(au0[:, 0:8192], wq_d),
                             writes=[bAuA], sem="wa0", n=16)
                        step("gpsimd", warena(au0[:, 8192:16384], wk_d),
                             writes=[bAuB], sem="wa1", n=16)
                    for ti in range(4):
                        tt = slab * 4 + ti
                        if e == 7 and b < B - 1:
                            xn_pre(b + 1, tt, (2 * tt + 1) % 3)
                        m2_tile(e, tt)
                        if e in (5, 6) and b < B - 1 and ti % 2 == 0:
                            idx = (e - 5) * 8 + slab * 2 + ti // 2
                            ln_stats_tile(b + 1, idx, idx % 3)
                        if e == 7 and b < B - 1:
                            xn_post(b + 1, tt)
                if e == 6 and b < B - 1:
                    stats_math()
                if e < NE - 1:
                    load_w2(b, e + 1)
                if e == NE - 1 and b < B - 1:
                    step("gpsimd", warena(au1[:, 0:8192], wv_d),
                         writes=[bAu1a], sem="wb0", n=16)
                    step("gpsimd", warena(au1[:, 8192:16384], wo_d),
                         writes=[bAu1b], sem="wb1", n=16)

            if b < B - 1:
                for tt in range(TT):
                    combine_tile(b, tt, (2 * tt) % 3)
                    if tt % 2 == 1 and tt <= 7:
                        k_quarter((tt - 1) // 2)
                    if tt >= 8:
                        j = tt - 8
                        v_tile(2 * j)
                        v_tile(2 * j + 1)
                        gate_tile(2 * j)
                        gate_tile(2 * j + 1)
                attn_qjit(0)
                attn_scores(0)
            else:
                for tt in range(TT):
                    combine_tile(b, tt, (2 * tt) % 3)

        em.emit("sync", lambda sp: None,
                waits=[(s, em.counts[s]) for s in ("sta0", "sta1", "sta2")
                       if s in em.counts])

        block = ctx.enter_context(nc.Block())

        def runner(ename):
            def run(eng):
                for (waits, fn, inc) in em.ops[ename]:
                    for (s, v) in waits:
                        eng.wait_ge(sems[s], v)
                    instr = fn(eng)
                    if instr is not None:
                        instr.then_inc(sems[inc[0]], inc[1])
            return run

        block.tensor(runner("tensor"))
        block.vector(runner("vector"))
        block.scalar(runner("scalar"))
        block.gpsimd(runner("gpsimd"))
        block.sync(runner("sync"))

    return nc


_CACHE = {}


def kernel(L0_attn_x, L0_attn_W_qkv, L0_attn_W_out, L0_ffn_gate_W,
           L0_ffn_W1, L0_ffn_b1, L0_ffn_W2, L0_ffn_b2):
    x = np.ascontiguousarray(np.asarray(L0_attn_x, dtype=np.float32))
    wqkv = np.asarray(L0_attn_W_qkv, dtype=np.float32)
    wqT = np.ascontiguousarray(wqkv[0::3, :].T)
    wkT = np.ascontiguousarray(wqkv[1::3, :].T)
    wvT = np.ascontiguousarray(wqkv[2::3, :].T)
    woT = np.ascontiguousarray(np.asarray(L0_attn_W_out, dtype=np.float32).T)
    gw = np.ascontiguousarray(np.asarray(L0_ffn_gate_W, dtype=np.float32))
    w1 = np.ascontiguousarray(np.asarray(L0_ffn_W1, dtype=np.float32))
    b1 = np.ascontiguousarray(np.asarray(L0_ffn_b1, dtype=np.float32))
    w2 = np.ascontiguousarray(np.asarray(L0_ffn_W2, dtype=np.float32))
    b2 = np.ascontiguousarray(np.asarray(L0_ffn_b2, dtype=np.float32))

    if "nc" not in _CACHE:
        _CACHE["nc"] = build_program()
    nc = _CACHE["nc"]

    shared = {"wqT": wqT, "wkT": wkT, "wvT": wvT, "woT": woT, "gw": gw,
              "w1": w1, "b1": b1, "w2": w2, "b2": b2}
    in_maps = []
    for c in range(NCORES):
        m = dict(shared)
        m["x"] = np.ascontiguousarray(x[c * B:(c + 1) * B])
        in_maps.append(m)
    res = run_bass_kernel_spmd(nc, in_maps, core_ids=list(range(NCORES)))
    out = np.concatenate([np.asarray(r["out"]) for r in res.results], axis=0)
    return out.astype(np.float32, copy=False)







# version 22
# speedup vs baseline: 1.1057x; 1.1057x over previous
import numpy as np
import concourse.bass as bass
import concourse.mybir as mybir
from concourse.bass_utils import run_bass_kernel_spmd

F32 = mybir.dt.float32
BF16 = mybir.dt.bfloat16
AF = mybir.ActivationFunctionType
ALU = mybir.AluOpType
AX = mybir.AxisListType.X

NCORES = 8
B = 4
S = 2048
H = 1024
DH = 2048
NE = 8
TT = S // 128
HC = H // 128
HIDC = DH // 128

ENGSEM = {"tensor": "pe", "vector": "dve", "scalar": "act",
          "gpsimd": "gp", "sync": "sp"}


class Emitter:
    def __init__(self):
        self.ops = {e: [] for e in ENGSEM}
        self.counts = {}
        self.last_wait = {}

    def emit(self, engine, fn, waits=(), sem=None, n=1):
        sem = sem or ENGSEM[engine]
        w = []
        for (s, v) in waits:
            if engine == "tensor" and s == "pe":
                continue
            have = self.counts.get(s, 0)
            assert v <= have, f"deadlock: {engine} waits {s}>={v} but built {have}"
            key = (engine, s)
            if self.last_wait.get(key, -1) < v:
                self.last_wait[key] = v
                w.append((s, v))
        self.counts[sem] = self.counts.get(sem, 0) + n
        tok = (sem, self.counts[sem])
        self.ops[engine].append((w, fn, (sem, n)))
        return tok


class Buf:

    def __init__(self, name):
        self.name = name
        self.wtok = None
        self.rtoks = []
        self.aliases = []

    def r_waits(self):
        return [self.wtok] if self.wtok else []

    def w_waits(self):
        ws = []
        for t in [self] + self.aliases:
            if t.wtok:
                ws.append(t.wtok)
            ws += t.rtoks
        return ws

    def did_read(self, tok):
        self.rtoks.append(tok)

    def did_write(self, tok):
        self.wtok = tok
        self.rtoks = []
        for t in self.aliases:
            t.wtok = None
            t.rtoks = []


def link_alias(a, b):
    if b not in a.aliases:
        a.aliases.append(b)
    if a not in b.aliases:
        b.aliases.append(a)


def build_program():
    nc = bass.Bass("TRN2", target_bir_lowering=False)
    em = Emitter()

    x_d = nc.dram_tensor("x", [B, S, H], F32, kind="ExternalInput").ap()
    wq_d = nc.dram_tensor("wqT", [H, H], F32, kind="ExternalInput").ap()
    wk_d = nc.dram_tensor("wkT", [H, H], F32, kind="ExternalInput").ap()
    wv_d = nc.dram_tensor("wvT", [H, H], F32, kind="ExternalInput").ap()
    wo_d = nc.dram_tensor("woT", [H, H], F32, kind="ExternalInput").ap()
    gw_d = nc.dram_tensor("gw", [H, NE], F32, kind="ExternalInput").ap()
    w1_d = nc.dram_tensor("w1", [NE, H, DH], F32, kind="ExternalInput").ap()
    b1_d = nc.dram_tensor("b1", [NE, DH], F32, kind="ExternalInput").ap()
    w2_d = nc.dram_tensor("w2", [NE, DH, H], F32, kind="ExternalInput").ap()
    b2_d = nc.dram_tensor("b2", [NE, H], F32, kind="ExternalInput").ap()
    out_d = nc.dram_tensor("out", [B, S, H], F32, kind="ExternalOutput").ap()
    sscr_d = nc.dram_tensor("s_scr", [S, H], BF16, kind="Internal").ap()
    w1s_d = nc.dram_tensor("w1s", [NE, 128, 16384], BF16, kind="Internal").ap()
    w2s_d = nc.dram_tensor("w2s", [NE, 128, 16384], BF16, kind="Internal").ap()

    from contextlib import ExitStack
    with ExitStack() as ctx:
        sb = lambda name, shape, dt: ctx.enter_context(nc.sbuf_tensor(name, shape, dt))
        au0 = sb("au0", [128, 16384], BF16)
        au1 = sb("au1", [128, 16384], BF16)
        xn_fm = sb("xn_fm", [128, HC, S], BF16)
        reg = sb("reg", [128, 40960], BF16)
        attn_sb = sb("attn_sb", [128, 2048], BF16)
        attnT_sb = sb("attnT_sb", [128, 2048], BF16)
        q_sb = sb("q_sb", [128, 1024], BF16)
        ctxT_own = sb("ctxT_own", [128, 1024], BF16)
        s_sb = sb("s_sb", [128, 2, 1024], BF16)
        xa = sb("xa", [128, 3, 1024], F32)
        cw_sb = sb("cw_sb", [128, TT * NE], F32)
        b2_sb = sb("b2_sb", [8, H], BF16)
        gw_sb = sb("gw_sb", [128, HC, NE], BF16)
        b1c_sb = sb("b1c_sb", [128, NE * HIDC], F32)
        id_bf = sb("id_bf", [128, 128], BF16)
        id_f32 = sb("id_f32", [128, 8], F32)
        msum = sb("msum", [128, TT], F32)
        sqs = sb("sqs", [128, TT], F32)
        nmean = msum
        rstd = sb("rstd", [128, TT], F32)
        nmr = sb("nmr", [128, TT], F32)
        gsc = sb("gsc", [128, 8], F32)
        e_sb = sb("e_sb", [128, NE], F32)
        eq_sb = sb("eq_sb", [128, NE], F32)
        mk_sb = eq_sb
        cwbf_sb = sb("cwbf_sb", [128, 2, NE], BF16)
        asc = sb("asc", [128, 12], F32)
        c32_sb = gsc[:, 7:8]

        rem = nc.sbuf_bytes_remaining
        assert rem >= 0, f"SBUF over budget by {-rem} bytes"

        pb = [ctx.enter_context(nc.psum_tensor(f"pb{i}", [128, 512], F32))
              for i in range(8)]
        pbf = [p.ap().bitcast(BF16) for p in pb]

        sems = {name: ctx.enter_context(nc.semaphore(name))
                for name in ["pe", "dve", "act", "gp", "sp",
                             "wa0", "wa1", "wb0", "wb1",
                             "lqa0", "lqa1", "lqa2",
                             "lqs0", "lqs1",
                             "sta0", "sta1", "sta2", "sts0", "sts1",
                             "w_b2", "w_gw", "w_b1", "wc"]}

        k_fm = reg[:, 0:16384].rearrange("p (tq hc t) -> p tq hc t", tq=4, hc=HC)
        v_tm = reg[:, 16384:32768].rearrange("p (tc h) -> p tc h", tc=TT)
        acc_f = reg.ap().bitcast(F32)
        acc = acc_f[:, 0:16384].rearrange("p (tt o) -> p tt o", tt=TT)
        h_fm = reg[:, 32768:40960].rearrange("p (hd t) -> p hd t", hd=HIDC)
        sq_f = attn_sb.ap().bitcast(F32)
        g_f = attnT_sb.ap().bitcast(F32)
        ctx_sb = attn_sb[:, 0:1024]
        ctxT_sb = ctxT_own.ap()
        cwtj_sb = ctxT_own[0:8, 0:128]

        bQ = Buf("q")
        bHa, bHb = Buf("ha"), Buf("hb")
        bKq = [Buf(f"kq{i}") for i in range(4)]
        bVp = [Buf(f"vp{i}") for i in range(8)]
        bAcc = [Buf(f"acc{i}") for i in range(TT)]
        for i in range(8):
            link_alias(bAcc[i], bKq[i // 2])
        for i in range(8, TT):
            link_alias(bAcc[i], bVp[i - 8])
        bXnQ = [Buf(f"xn{i}") for i in range(4)]
        bAtL, bAtR = Buf("attnL"), Buf("attnR")
        bAttnT = Buf("attnT")
        bAtTA, bAtTB = Buf("attnTA"), Buf("attnTB")
        link_alias(bAtTA, bAttnT)
        link_alias(bAtTB, bAttnT)
        bCtxTo = Buf("ctxTo")
        bS = [Buf("s0"), Buf("s1")]
        bXa = [Buf(f"xa{i}") for i in range(3)]
        bCw, bCwTj = Buf("cw"), Buf("cwTj")
        link_alias(bCwTj, bCtxTo)
        bB2, bGw, bB1c = Buf("b2"), Buf("gw"), Buf("b1c")
        bIdb, bIdf = Buf("idb"), Buf("idf")
        bAuA, bAuB = Buf("au0a"), Buf("au0b")
        bAu1a, bAu1b = Buf("au1a"), Buf("au1b")
        bStat = Buf("stat")
        bGsc = Buf("gsc")
        bCwb = [Buf("cwb0"), Buf("cwb1")]
        bAsc = Buf("asc")
        bPB = [Buf(f"pb{i}") for i in range(8)]
        bSscr = [Buf(f"s_scr{i}") for i in range(TT)]
        bW1s = [Buf(f"w1s{i}") for i in range(NE)]
        bW2s = [Buf(f"w2s{i}") for i in range(NE)]

        def step(engine, fn, reads=(), writes=(), extra=(), sem=None, n=1):
            waits = list(extra)
            for bf in reads:
                waits += bf.r_waits()
            for bf in writes:
                waits += bf.w_waits()
            tok = em.emit(engine, fn, waits, sem=sem, n=n)
            for bf in reads:
                bf.did_read(tok)
            for bf in writes:
                bf.did_write(tok)
            return tok

        nc.const_aps.register(F32, 1.0 / 32.0, c32_sb)
        step("gpsimd", lambda g: g.memset(c32_sb, 1.0 / 32.0),
             writes=[bIdf])
        step("gpsimd", lambda g: g.memset(id_bf[:, :], 0.0), writes=[bIdb])
        step("gpsimd", lambda g: g.affine_select(
            out=id_bf[:, :], in_=id_bf[:, :], compare_op=ALU.not_equal,
            fill=1.0, base=0, pattern=[[-1, 128]], channel_multiplier=1),
            reads=[bIdb], writes=[bIdb])
        step("gpsimd", lambda g: g.memset(id_f32[:, :], 0.0), writes=[bIdf])
        step("gpsimd", lambda g: g.affine_select(
            out=id_f32[:, :], in_=id_f32[:, :], compare_op=ALU.not_equal,
            fill=1.0, base=0, pattern=[[-1, 8]], channel_multiplier=1),
            reads=[bIdf], writes=[bIdf])

        step("gpsimd", lambda g: g.dma_start(out=b2_sb[:, :], in_=b2_d),
             writes=[bB2], sem="w_b2", n=16)
        step("gpsimd", lambda g: g.dma_start(
            out=gw_sb[:, :, :],
            in_=gw_d.rearrange("(hc p) e -> p hc e", p=128)),
            writes=[bGw], sem="w_gw", n=16)
        b1nat = xa.ap().rearrange("p three h -> p (three h)")[0:8, 0:2048]
        step("gpsimd", lambda g: g.dma_start(out=b1nat, in_=b1_d),
             writes=[bXa[0], bXa[1]], sem="w_b1", n=16)

        def f_b1t(t):
            last = None
            for hd in range(HIDC):
                last = t.transpose(pb[6][:, hd * 8:(hd + 1) * 8],
                                   b1nat[0:8, hd * 128:(hd + 1) * 128],
                                   id_f32[0:8, 0:8])
            return last
        step("tensor", f_b1t, reads=[bXa[0], bXa[1], bIdf], writes=[bPB[6]])
        step("vector", lambda v: v.tensor_copy(
            b1c_sb[:, :].rearrange("p (e hd) -> p e hd", e=NE),
            pb[6][:, 0:128].rearrange("p (hd e) -> p e hd", hd=HIDC)),
            reads=[bPB[6]], writes=[bB1c])

        def warena(dst, src):
            return lambda g: g.dma_start(
                out=dst.rearrange("p (hc o) -> p hc o", hc=HC),
                in_=src.rearrange("(hc p) o -> p hc o", p=128))
        step("gpsimd", warena(au0[:, 0:8192], wq_d), writes=[bAuA], sem="wa0", n=16)
        step("gpsimd", warena(au0[:, 8192:16384], wk_d), writes=[bAuB], sem="wa1", n=16)
        step("gpsimd", warena(au1[:, 0:8192], wv_d), writes=[bAu1a], sem="wb0", n=16)
        step("gpsimd", warena(au1[:, 8192:16384], wo_d), writes=[bAu1b], sem="wb1", n=16)
        for ci in range(2 * NE):
            e = ci % NE
            if ci < NE:
                fn = (lambda e=e: lambda g: g.dma_start(
                    out=w1s_d[e].rearrange("p (hc d) -> p hc d", hc=HC),
                    in_=w1_d[e].rearrange("(hc p) d -> p hc d", p=128)))()
                trk = bW1s[e]
            else:
                fn = (lambda e=e: lambda g: g.dma_start(
                    out=w2s_d[e].rearrange("p (hd o) -> p hd o", hd=HIDC),
                    in_=w2_d[e].rearrange("(hd p) o -> p hd o", p=128)))()
                trk = bW2s[e]
            step("gpsimd", fn, writes=[trk], extra=[("wc", 16 * ci)],
                 sem="wc", n=16)

        def ln_stats_tile(b, i, sl):
            step("sync", (lambda: lambda sp: sp.dma_start(
                out=xa[:, sl, :], in_=x_d[b, i * 128:(i + 1) * 128, :]))(),
                writes=[bXa[sl]], sem="lqa%d" % sl, n=16)
            step("vector", (lambda: lambda v: v.reduce_sum(
                msum[:, i:i + 1], xa[:, sl, :], AX))(),
                reads=[bXa[sl]], writes=[bStat])
            step("scalar", (lambda: lambda s: s.activation(
                sq_f[:, :], xa[:, sl, :], AF.Square,
                accum_out=sqs[:, i:i + 1]))(),
                reads=[bXa[sl]], writes=[bAtL, bAtR, bStat])

        def stats_math():
            step("vector", lambda v: v.tensor_scalar(
                nmean[:, :], msum[:, :], -1.0 / H, None, ALU.mult),
                reads=[bStat], writes=[bStat])
            step("vector", lambda v: v.tensor_mul(
                rstd[:, :], nmean[:, :], nmean[:, :]),
                reads=[bStat], writes=[bStat])
            step("vector", lambda v: v.scalar_tensor_tensor(
                rstd[:, :], sqs[:, :], 1.0 / H, rstd[:, :],
                ALU.mult, ALU.subtract),
                reads=[bStat], writes=[bStat])
            step("vector", lambda v: v.tensor_scalar_add(
                rstd[:, :], rstd[:, :], 1e-5),
                reads=[bStat], writes=[bStat])
            step("scalar", lambda s: s.activation(
                sqs[:, :], rstd[:, :], AF.Sqrt),
                reads=[bStat], writes=[bStat])
            step("vector", lambda v: v.reciprocal(rstd[:, :], sqs[:, :]),
                 reads=[bStat], writes=[bStat])
            step("vector", lambda v: v.tensor_mul(
                nmr[:, :], nmean[:, :], rstd[:, :]),
                reads=[bStat], writes=[bStat])

        def xn_pre(b, i, sl):
            half = i % 2
            xntm = attn_sb[:, half * 1024:(half + 1) * 1024]
            bHalf = bAtL if half == 0 else bAtR
            step("sync", (lambda: lambda sp: sp.dma_start(
                out=xa[:, sl, :], in_=x_d[b, i * 128:(i + 1) * 128, :]))(),
                writes=[bXa[sl]], sem="lqa%d" % sl, n=16)
            step("vector", (lambda: lambda v: v.tensor_scalar(
                xntm, xa[:, sl, :], rstd[:, i:i + 1], nmr[:, i:i + 1],
                ALU.mult, ALU.add))(),
                reads=[bXa[sl], bStat], writes=[bHalf])

        def xn_post(b, i):
            half = i % 2
            xntm = attn_sb[:, half * 1024:(half + 1) * 1024]
            bHalf = bAtL if half == 0 else bAtR
            bank = 6 + half

            def f_xt(t):
                last = None
                for hc in range(HC):
                    last = t.transpose(pbf[bank][:, hc * 128:(hc + 1) * 128],
                                       xntm[:, hc * 128:(hc + 1) * 128],
                                       id_bf[:, :])
                return last
            step("tensor", f_xt, reads=[bHalf, bIdb], writes=[bPB[bank]])
            step("vector", (lambda: lambda v: v.tensor_copy(
                xn_fm[:, :, i * 128:(i + 1) * 128], pbf[bank][:, 0:1024]
                .rearrange("p (hc t) -> p hc t", hc=HC)))(),
                reads=[bPB[bank]], writes=[bXnQ[i // 4]])

        def xn_tile(b, i, sl):
            xn_pre(b, i, sl)
            xn_post(b, i)

        def k_quarter(tq):
            for oc in range(HC):
                bank = oc % 2

                def f_mm(t, oc=oc, bank=bank):
                    last = None
                    for hc in range(HC):
                        last = t.matmul(
                            pb[bank][:, :],
                            au0[:, 8192 + hc * 1024 + oc * 128:
                                8192 + hc * 1024 + oc * 128 + 128],
                            xn_fm[:, hc, tq * 512:(tq + 1) * 512],
                            start=(hc == 0), stop=(hc == HC - 1))
                    return last
                step("tensor", f_mm, reads=[bAuB, bXnQ[tq]], writes=[bPB[bank]])
                step("scalar", (lambda oc=oc, bank=bank: lambda s: s.copy(
                    k_fm[:, tq, oc, :], pb[bank][:, :]))(),
                    reads=[bPB[bank]], writes=[bKq[tq]])

        def v_tile(tt):
            for half in range(2):
                bank = 2 + half

                def f_mm(t, half=half, bank=bank):
                    last = None
                    for hc in range(HC):
                        last = t.matmul(
                            pb[bank][:, :],
                            xn_fm[:, hc, tt * 128:(tt + 1) * 128],
                            au1[:, hc * 1024 + half * 512:
                                hc * 1024 + half * 512 + 512],
                            start=(hc == 0), stop=(hc == HC - 1))
                    return last
                step("tensor", f_mm, reads=[bAu1a, bXnQ[tt // 4]],
                     writes=[bPB[bank]])
                step("scalar", (lambda half=half, bank=bank: lambda s: s.copy(
                    v_tm[:, tt, half * 512:(half + 1) * 512], pb[bank][:, :]))(),
                    reads=[bPB[bank]], writes=[bVp[tt // 2]])

        def gate_tile(tt):
            gb = 6 + tt % 2

            def f_gmm(t):
                last = None
                for hc in range(HC):
                    last = t.matmul(
                        pb[gb][:, 0:NE],
                        xn_fm[:, hc, tt * 128:(tt + 1) * 128],
                        gw_sb[:, hc, :],
                        start=(hc == 0), stop=(hc == HC - 1))
                return last
            step("tensor", f_gmm, reads=[bGw, bXnQ[tt // 4]], writes=[bPB[gb]])
            step("vector", (lambda gb=gb: lambda v: v.reduce_max(
                gsc[:, 0:1], pb[gb][:, 0:NE], AX, negate=True))(),
                reads=[bPB[gb]], writes=[bGsc])
            step("scalar", (lambda gb=gb: lambda s: s.activation(
                e_sb[:, :], pb[gb][:, 0:NE], AF.Exp,
                bias=gsc[:, 0:1], accum_out=gsc[:, 1:2]))(),
                reads=[bPB[gb], bGsc], writes=[bGsc])
            step("vector", lambda v: v.reduce_max(gsc[:, 2:3], e_sb[:, :], AX),
                 reads=[bGsc], writes=[bGsc])
            step("vector", lambda v: v.tensor_scalar(
                eq_sb[:, :], e_sb[:, :], gsc[:, 2:3], None, ALU.is_ge),
                reads=[bGsc], writes=[bGsc])
            step("vector", lambda v: v.scalar_tensor_tensor(
                mk_sb[:, :], eq_sb[:, :], -1e30, e_sb[:, :], ALU.mult, ALU.add),
                reads=[bGsc], writes=[bGsc])
            step("vector", lambda v: v.reduce_max(gsc[:, 3:4], mk_sb[:, :], AX),
                 reads=[bGsc], writes=[bGsc])
            step("vector", lambda v: v.tensor_scalar(
                eq_sb[:, :], e_sb[:, :], gsc[:, 3:4], None, ALU.is_ge),
                reads=[bGsc], writes=[bGsc])
            step("vector", lambda v: v.scalar_tensor_tensor(
                gsc[:, 4:5], gsc[:, 1:2], 1e-8, gsc[:, 2:3], ALU.mult, ALU.add),
                reads=[bGsc], writes=[bGsc])
            step("vector", lambda v: v.tensor_add(
                gsc[:, 5:6], gsc[:, 4:5], gsc[:, 3:4]),
                reads=[bGsc], writes=[bGsc])
            step("vector", lambda v: v.reciprocal(gsc[:, 6:7], gsc[:, 5:6]),
                 reads=[bGsc], writes=[bGsc])
            step("vector", (lambda: lambda v: v.tensor_scalar(
                cw_sb[:, tt * NE:(tt + 1) * NE], e_sb[:, :], gsc[:, 6:7],
                None, ALU.mult))(),
                reads=[bGsc], writes=[bCw])
            step("vector", (lambda: lambda v: v.tensor_mul(
                cw_sb[:, tt * NE:(tt + 1) * NE],
                cw_sb[:, tt * NE:(tt + 1) * NE], eq_sb[:, :]))(),
                reads=[bGsc, bCw], writes=[bCw])

        def attn_qjit(qt):
            def f_q(t):
                last = None
                for oc in range(HC):
                    bank = 6 + oc // 4
                    for hc in range(HC):
                        last = t.matmul(
                            pb[bank][:, (oc % 4) * 128:(oc % 4) * 128 + 128],
                            au0[:, hc * 1024 + oc * 128:
                                hc * 1024 + oc * 128 + 128],
                            xn_fm[:, hc, qt * 128:(qt + 1) * 128],
                            start=(hc == 0), stop=(hc == HC - 1))
                return last
            step("tensor", f_q, reads=[bAuA, bXnQ[qt // 4]],
                 writes=[bPB[6], bPB[7]])
            for bk in (6, 7):
                step("scalar", (lambda bk=bk: lambda s: s.activation(
                    q_sb[:, (bk - 6) * 512:(bk - 6) * 512 + 512],
                    pb[bk][:, :], AF.Copy, scale=1.0 / 32.0))(),
                    reads=[bPB[bk], bIdf], writes=[bQ])

        def attn_scores(qt):
            def f_sc(lo):
                def f(t):
                    last = None
                    for kq in (lo, lo + 1):
                        for oc in range(HC):
                            last = t.matmul(
                                pb[kq][:, :],
                                q_sb[:, oc * 128:(oc + 1) * 128],
                                k_fm[:, kq, oc, :],
                                start=(oc == 0), stop=(oc == HC - 1))
                    return last
                return f
            step("tensor", f_sc(0), reads=[bQ, bKq[0], bKq[1]],
                 writes=[bPB[0], bPB[1]])
            for kq in (0, 1):
                step("vector", (lambda kq=kq: lambda v: v.reduce_max(
                    asc[:, kq:kq + 1], pb[kq][:, :], AX, negate=True))(),
                    reads=[bPB[kq]], writes=[bAsc])
            step("tensor", f_sc(2), reads=[bQ, bKq[2], bKq[3]],
                 writes=[bPB[2], bPB[3]])
            for kq in (2, 3):
                step("vector", (lambda kq=kq: lambda v: v.reduce_max(
                    asc[:, kq:kq + 1], pb[kq][:, :], AX, negate=True))(),
                    reads=[bPB[kq]], writes=[bAsc])
            step("vector", lambda v: v.tensor_reduce(
                asc[:, 4:5], asc[:, 0:4], AX, ALU.min),
                reads=[bAsc], writes=[bAsc])
            for kq in range(4):
                bHalf = bAtL if kq < 2 else bAtR
                step("scalar", (lambda kq=kq: lambda s: s.activation(
                    attn_sb[:, kq * 512:(kq + 1) * 512], pb[kq][:, :],
                    AF.Exp, bias=asc[:, 4:5],
                    accum_out=asc[:, 6 + kq:7 + kq]))(),
                    reads=[bPB[kq], bAsc], writes=[bHalf, bAsc])

        def attn_rest(qt):
            step("vector", lambda v: v.tensor_reduce(
                asc[:, 10:11], asc[:, 6:10], AX, ALU.add),
                reads=[bAsc], writes=[bAsc])
            step("vector", lambda v: v.reciprocal(asc[:, 11:12], asc[:, 10:11]),
                 reads=[bAsc], writes=[bAsc])
            if qt < TT - 1:
                attn_qjit(qt + 1)
            for bk, lo in ((4, 0), (5, 8)):
                bHalf = bAtL if lo == 0 else bAtR

                def f_tr(t, bk=bk, lo=lo):
                    last = None
                    for c in range(8):
                        ktc = lo + c
                        last = t.transpose(
                            pbf[bk][:, c * 128:(c + 1) * 128],
                            attn_sb[:, ktc * 128:(ktc + 1) * 128],
                            id_bf[:, :])
                    return last
                step("tensor", f_tr, reads=[bHalf, bIdb], writes=[bPB[bk]])
                step("scalar", (lambda bk=bk, lo=lo: lambda s: s.copy(
                    attnT_sb[:, lo * 128:(lo + 8) * 128], pbf[bk][:, 0:1024]))(),
                    reads=[bPB[bk]], writes=[bAtTA if lo == 0 else bAtTB])

            def f_ctx(lo):
                def f(t):
                    last = None
                    for half in range(2):
                        for ktc in range(lo, lo + 8):
                            last = t.matmul(
                                pb[6 + half][:, :],
                                attnT_sb[:, ktc * 128:(ktc + 1) * 128],
                                v_tm[:, ktc, half * 512:(half + 1) * 512],
                                start=(ktc == 0), stop=(ktc == TT - 1))
                    return last
                return f
            step("tensor", f_ctx(0), reads=[bAtTA] + bVp, writes=[bPB[6], bPB[7]])
            step("tensor", f_ctx(8), reads=[bAtTB] + bVp, writes=[bPB[6], bPB[7]])
            for half in range(2):
                step("vector", (lambda half=half: lambda v: v.tensor_scalar(
                    ctx_sb[:, half * 512:(half + 1) * 512],
                    pb[6 + half][:, :], asc[:, 11:12], None, ALU.mult))(),
                    reads=[bPB[6 + half], bAsc], writes=[bAtL])

            def f_ctr(t):
                last = None
                for hc in range(HC):
                    last = t.transpose(
                        pbf[5][:, hc * 128:(hc + 1) * 128],
                        ctx_sb[:, hc * 128:(hc + 1) * 128], id_bf[:, :])
                return last
            step("tensor", f_ctr, reads=[bAtL, bIdb], writes=[bPB[5]])
            step("scalar", lambda s: s.copy(
                ctxT_sb[:, :], pbf[5][:, 0:1024]),
                reads=[bPB[5]], writes=[bCtxTo])

            if qt < TT - 1:
                attn_scores(qt + 1)

            def f_s(t):
                last = None
                for half in range(2):
                    bank = (7, 4)[half]
                    for hc in range(HC):
                        last = t.matmul(
                            pb[bank][:, :],
                            ctxT_sb[:, hc * 128:(hc + 1) * 128],
                            au1[:, 8192 + hc * 1024 + half * 512:
                                8192 + hc * 1024 + half * 512 + 512],
                            start=(hc == 0), stop=(hc == HC - 1))
                return last
            step("tensor", f_s, reads=[bCtxTo, bAu1b], writes=[bPB[7], bPB[4]])
            sl = qt % 2
            for half in range(2):
                bank = (7, 4)[half]
                step("vector", (lambda half=half, bank=bank: lambda v: v.tensor_copy(
                    s_sb[:, sl, half * 512:(half + 1) * 512], pb[bank][:, :]))(),
                    reads=[bPB[bank]], writes=[bS[sl]])
            step("sync", (lambda: lambda sp: sp.dma_start(
                out=sscr_d[qt * 128:(qt + 1) * 128, :], in_=s_sb[:, sl, :]))(),
                reads=[bS[sl]], writes=[bSscr[qt]], sem="sts%d" % sl, n=16)

        def cw_copy(tt):
            step("vector", (lambda: lambda v: v.tensor_copy(
                cwbf_sb[:, tt % 2, :], cw_sb[:, tt * NE:(tt + 1) * NE]))(),
                reads=[bCw], writes=[bCwb[tt % 2]])

        def accinit_tile(b, tt):
            sl = tt % 3
            step("tensor", (lambda: lambda t: t.transpose(
                pbf[6][:, 0:128][0:NE, :], cwbf_sb[:, tt % 2, :],
                id_bf[:, :]))(),
                reads=[bCwb[tt % 2], bIdb], writes=[bPB[6]])
            step("vector", lambda v: v.tensor_copy(
                cwtj_sb[:, :], pbf[6][0:NE, 0:128]),
                reads=[bPB[6]], writes=[bCwTj])

            def f_b2m(t):
                t.matmul(pb[6][:, :], cwtj_sb[0:NE, :],
                         b2_sb[0:NE, 0:512], start=True, stop=True)
                return t.matmul(pb[7][:, :], cwtj_sb[0:NE, :],
                                b2_sb[0:NE, 512:1024], start=True, stop=True)
            step("tensor", f_b2m, reads=[bCwTj, bB2], writes=[bPB[6], bPB[7]])
            step("sync", (lambda: lambda sp: sp.dma_start(
                out=xa[:, sl, :], in_=x_d[b, tt * 128:(tt + 1) * 128, :]))(),
                writes=[bXa[sl]], sem="lqa%d" % sl, n=16)
            step("vector", (lambda: lambda v: v.tensor_scalar(
                acc[:, tt, :], xa[:, sl, :], rstd[:, tt:tt + 1],
                nmr[:, tt:tt + 1], ALU.mult, ALU.add))(),
                reads=[bXa[sl], bStat], writes=[bAcc[tt]])
            for half in range(2):
                step("vector", (lambda half=half: lambda v: v.tensor_add(
                    acc[:, tt, half * 512:(half + 1) * 512],
                    acc[:, tt, half * 512:(half + 1) * 512],
                    pb[6 + half][:, :]))(),
                    reads=[bPB[6 + half], bAcc[tt]], writes=[bAcc[tt]])
            if tt < TT - 1:
                cw_copy(tt + 1)

        def m1_group(e, slab, hd):
            bank = (slab * HIDC + hd) % 2

            def f_m1(t):
                last = None
                for hc in range(HC):
                    last = t.matmul(
                        pb[bank][:, :],
                        au0[:, hc * 2048 + hd * 128:hc * 2048 + hd * 128 + 128],
                        xn_fm[:, hc, slab * 512:(slab + 1) * 512],
                        start=(hc == 0), stop=(hc == HC - 1))
                return last
            step("tensor", f_m1, reads=[bAuA, bAuB, bXnQ[slab]],
                 writes=[bPB[bank]])
            trk = bHa if hd < 8 else bHb
            if hd == HIDC - 1:
                step("vector", (lambda: lambda v: v.tensor_scalar(
                    h_fm[:, hd, :], pb[bank][:, :],
                    b1c_sb[:, e * HIDC + hd:e * HIDC + hd + 1], 0.0,
                    ALU.add, ALU.max))(),
                    reads=[bPB[bank], bB1c], writes=[trk])
            else:
                step("scalar", (lambda: lambda s: s.activation(
                    h_fm[:, hd, :], pb[bank][:, :], AF.Relu,
                    bias=b1c_sb[:, e * HIDC + hd:e * HIDC + hd + 1]))(),
                    reads=[bPB[bank], bB1c], writes=[trk])

        def m2_tile(e, tt):
            if 1 <= e <= 6:
                pr = (2, 4, 6)[tt % 3]
            else:
                pr = 2 + 2 * (tt % 2)

            def f_m2(lo):
                def f(t):
                    last = None
                    for half in range(2):
                        for hd in range(lo, lo + 8):
                            last = t.matmul(
                                pb[pr + half][:, :],
                                h_fm[:, hd, (tt % 4) * 128:(tt % 4) * 128 + 128],
                                au1[:, hd * 1024 + half * 512:
                                    hd * 1024 + half * 512 + 512],
                                start=(hd == 0), stop=(hd == HIDC - 1))
                    return last
                return f
            step("tensor", f_m2(0), reads=[bHa, bAu1a, bAu1b],
                 writes=[bPB[pr], bPB[pr + 1]])
            step("tensor", f_m2(8), reads=[bHb, bAu1a, bAu1b],
                 writes=[bPB[pr], bPB[pr + 1]])
            for half in range(2):
                step("vector", (lambda half=half: lambda v: v.scalar_tensor_tensor(
                    acc[:, tt, half * 512:(half + 1) * 512],
                    pb[pr + half][:, :],
                    cw_sb[:, tt * NE + e:tt * NE + e + 1],
                    acc[:, tt, half * 512:(half + 1) * 512],
                    ALU.mult, ALU.add))(),
                    reads=[bPB[pr + half], bCw, bAcc[tt]],
                    writes=[bAcc[tt]])

        def combine_tile(b, tt, sl):
            ss = tt % 2
            gbuf = g_f if tt % 2 == 0 else sq_f
            gtrk = [bAttnT] if tt % 2 == 0 else [bAtL, bAtR]
            step("sync", (lambda: lambda sp: sp.dma_start(
                out=xa[:, sl, :], in_=x_d[b, tt * 128:(tt + 1) * 128, :]))(),
                writes=[bXa[sl]], sem="lqa%d" % sl, n=16)
            step("sync", (lambda: lambda sp: sp.dma_start(
                out=s_sb[:, ss, :], in_=sscr_d[tt * 128:(tt + 1) * 128, :]))(),
                reads=[bSscr[tt]], writes=[bS[ss]], sem="lqs%d" % ss, n=16)
            step("scalar", (lambda: lambda s: s.activation(
                gbuf[:, :], acc[:, tt, :], AF.Silu))(),
                reads=[bAcc[tt]], writes=gtrk)
            step("vector", (lambda: lambda v: v.tensor_mul(
                gbuf[:, :], gbuf[:, :], s_sb[:, ss, :]))(),
                reads=[bS[ss]] + gtrk, writes=gtrk)
            step("vector", (lambda: lambda v: v.tensor_add(
                xa[:, sl, :], xa[:, sl, :], gbuf[:, :]))(),
                extra=[t for bf in gtrk for t in bf.r_waits()],
                reads=gtrk + [bXa[sl]], writes=[bXa[sl]])
            step("sync", (lambda: lambda sp: sp.dma_start(
                out=out_d[b, tt * 128:(tt + 1) * 128, :], in_=xa[:, sl, :]))(),
                reads=[bXa[sl]], sem="sta%d" % sl, n=16)

        def load_w1_half(b, e, lohi):
            half = (au0[:, 0:8192], au0[:, 8192:16384])[lohi]
            trk = (bAuA, bAuB)[lohi]
            cols = (slice(0, 8192), slice(8192, 16384))[lohi]
            step("gpsimd", (lambda: lambda g: g.dma_start(
                out=half, in_=w1s_d[e][:, cols]))(),
                reads=[bW1s[e]], writes=[trk], sem="wa%d" % lohi, n=16)

        def load_w1_full(b, e):
            step("gpsimd", (lambda: lambda g: g.dma_start(
                out=au0[:, :], in_=w1s_d[e]))(),
                reads=[bW1s[e]], writes=[bAuA, bAuB], sem="wa0", n=16)

        def load_w2(b, e):
            step("gpsimd", (lambda: lambda g: g.dma_start(
                out=au1[:, :], in_=w2s_d[e]))(),
                reads=[bW2s[e]], writes=[bAu1a, bAu1b], sem="wb0", n=16)

        for i in range(TT):
            step("sync", (lambda i=i: lambda sp: sp.dma_start(
                out=acc[:, i, :], in_=x_d[0, i * 128:(i + 1) * 128, :]))(),
                writes=[bAcc[i]],
                extra=[("lqa%d" % (i % 3), 16 * (i // 3))],
                sem="lqa%d" % (i % 3), n=16)
            step("vector", (lambda i=i: lambda v: v.reduce_sum(
                msum[:, i:i + 1], acc[:, i, :], AX))(),
                reads=[bAcc[i]], writes=[bStat])
            step("scalar", (lambda i=i: lambda s: s.activation(
                sq_f[:, :], acc[:, i, :], AF.Square,
                accum_out=sqs[:, i:i + 1]))(),
                reads=[bAcc[i]], writes=[bAtL, bAtR, bStat])
        stats_math()
        for i in range(TT):
            half = i % 2
            xntm0 = attn_sb[:, half * 1024:(half + 1) * 1024]
            step("vector", (lambda i=i, xntm0=xntm0: lambda v: v.tensor_scalar(
                xntm0, acc[:, i, :], rstd[:, i:i + 1], nmr[:, i:i + 1],
                ALU.mult, ALU.add))(),
                reads=[bAcc[i], bStat], writes=[bAtL if half == 0 else bAtR])
            xn_post(0, i)
        for tq in range(4):
            k_quarter(tq)
        for tt in range(TT):
            v_tile(tt)
            gate_tile(tt)

        for b in range(B):
            load_w1_half(b, 0, 1)
            if b == 0:
                attn_qjit(0)
                attn_scores(0)
            for qt in range(TT):
                attn_rest(qt)
            load_w1_half(b, 0, 0)
            load_w2(b, 0)
            cw_copy(0)

            for e in range(NE):
                for slab in range(4):
                    for hd in range(HIDC):
                        m1_group(e, slab, hd)
                        if e == 0 and hd % 4 == 3:
                            accinit_tile(b, slab * 4 + hd // 4)
                    if slab == 3 and e < NE - 1:
                        load_w1_full(b, e + 1)
                    if slab == 3 and e == NE - 1 and b < B - 1:
                        step("gpsimd", warena(au0[:, 0:8192], wq_d),
                             writes=[bAuA], sem="wa0", n=16)
                        step("gpsimd", warena(au0[:, 8192:16384], wk_d),
                             writes=[bAuB], sem="wa1", n=16)
                    for ti in range(4):
                        tt = slab * 4 + ti
                        if e == 7 and b < B - 1:
                            xn_pre(b + 1, tt, (2 * tt + 1) % 3)
                        m2_tile(e, tt)
                        if e in (5, 6) and b < B - 1 and ti % 2 == 0:
                            idx = (e - 5) * 8 + slab * 2 + ti // 2
                            ln_stats_tile(b + 1, idx, idx % 3)
                        if e == 7 and b < B - 1:
                            xn_post(b + 1, tt)
                if e == 6 and b < B - 1:
                    stats_math()
                if e < NE - 1:
                    load_w2(b, e + 1)
                if e == NE - 1 and b < B - 1:
                    step("gpsimd", warena(au1[:, 0:8192], wv_d),
                         writes=[bAu1a], sem="wb0", n=16)
                    step("gpsimd", warena(au1[:, 8192:16384], wo_d),
                         writes=[bAu1b], sem="wb1", n=16)

            if b < B - 1:
                for tt in range(TT):
                    combine_tile(b, tt, (2 * tt) % 3)
                    if tt % 2 == 1 and tt <= 7:
                        k_quarter((tt - 1) // 2)
                    if tt >= 8:
                        j = tt - 8
                        v_tile(2 * j)
                        v_tile(2 * j + 1)
                        gate_tile(2 * j)
                        gate_tile(2 * j + 1)
                attn_qjit(0)
                attn_scores(0)
            else:
                for tt in range(TT):
                    combine_tile(b, tt, (2 * tt) % 3)

        em.emit("sync", lambda sp: None,
                waits=[(s, em.counts[s]) for s in ("sta0", "sta1", "sta2")
                       if s in em.counts])

        block = ctx.enter_context(nc.Block())

        def runner(ename):
            def run(eng):
                for (waits, fn, inc) in em.ops[ename]:
                    for (s, v) in waits:
                        eng.wait_ge(sems[s], v)
                    instr = fn(eng)
                    if instr is not None:
                        instr.then_inc(sems[inc[0]], inc[1])
            return run

        block.tensor(runner("tensor"))
        block.vector(runner("vector"))
        block.scalar(runner("scalar"))
        block.gpsimd(runner("gpsimd"))
        block.sync(runner("sync"))

    return nc


_CACHE = {}


def kernel(L0_attn_x, L0_attn_W_qkv, L0_attn_W_out, L0_ffn_gate_W,
           L0_ffn_W1, L0_ffn_b1, L0_ffn_W2, L0_ffn_b2):
    x = np.ascontiguousarray(np.asarray(L0_attn_x, dtype=np.float32))
    wqkv = np.asarray(L0_attn_W_qkv, dtype=np.float32)
    wqT = np.ascontiguousarray(wqkv[0::3, :].T)
    wkT = np.ascontiguousarray(wqkv[1::3, :].T)
    wvT = np.ascontiguousarray(wqkv[2::3, :].T)
    woT = np.ascontiguousarray(np.asarray(L0_attn_W_out, dtype=np.float32).T)
    gw = np.ascontiguousarray(np.asarray(L0_ffn_gate_W, dtype=np.float32))
    w1 = np.ascontiguousarray(np.asarray(L0_ffn_W1, dtype=np.float32))
    b1 = np.ascontiguousarray(np.asarray(L0_ffn_b1, dtype=np.float32))
    w2 = np.ascontiguousarray(np.asarray(L0_ffn_W2, dtype=np.float32))
    b2 = np.ascontiguousarray(np.asarray(L0_ffn_b2, dtype=np.float32))

    if "nc" not in _CACHE:
        _CACHE["nc"] = build_program()
    nc = _CACHE["nc"]

    shared = {"wqT": wqT, "wkT": wkT, "wvT": wvT, "woT": woT, "gw": gw,
              "w1": w1, "b1": b1, "w2": w2, "b2": b2}
    in_maps = []
    for c in range(NCORES):
        m = dict(shared)
        m["x"] = np.ascontiguousarray(x[c * B:(c + 1) * B])
        in_maps.append(m)
    res = run_bass_kernel_spmd(nc, in_maps, core_ids=list(range(NCORES)))
    out = np.concatenate([np.asarray(r["out"]) for r in res.results], axis=0)
    return out.astype(np.float32, copy=False)





